# version 06
# baseline (speedup 1.0000x reference)
import contextlib
import numpy as np
import ml_dtypes
import concourse.bass as bass
import concourse.mybir as mybir
from concourse.bass_utils import run_bass_kernel_spmd

F32 = mybir.dt.float32
BF16 = mybir.dt.bfloat16
I32 = mybir.dt.int32
AF = mybir.ActivationFunctionType
ALU = mybir.AluOpType

D = 2048
T = 2048
TC = 16
JC = 16
NOB = 4
HEADS = 16
INTER = 5504
NIC = 43
EPS_LN = 1e-5
EPS_Z = 1e-6


def build_nc(stage=99):
    nc = bass.Bass("TRN2", target_bir_lowering=False)

    x_hbm = nc.dram_tensor("x", [T, D], F32, kind="ExternalInput").ap()
    qw_t = nc.dram_tensor("qw_t", [16, JC, 128, 128], F32, kind="ExternalInput").ap()
    kw_t = nc.dram_tensor("kw_t", [NOB, JC, 128, 512], F32, kind="ExternalInput").ap()
    vw_t = nc.dram_tensor("vw_t", [NOB, JC, 128, 512], F32, kind="ExternalInput").ap()
    ow_t = nc.dram_tensor("ow_t", [NOB, JC, 128, 512], F32, kind="ExternalInput").ap()
    qb_h = nc.dram_tensor("qb", [D], F32, kind="ExternalInput").ap()
    kb_h = nc.dram_tensor("kb", [D], F32, kind="ExternalInput").ap()
    vb_h = nc.dram_tensor("vb", [D], F32, kind="ExternalInput").ap()
    ob_h = nc.dram_tensor("ob", [D], F32, kind="ExternalInput").ap()
    gi_h = nc.dram_tensor("g_idx", [256, INTER], I32, kind="ExternalInput").ap()
    ui_h = nc.dram_tensor("u_idx", [256, INTER], I32, kind="ExternalInput").ap()
    di_h = nc.dram_tensor("d_idx", [688, D], I32, kind="ExternalInput").ap()
    lut_h = nc.dram_tensor("lut", [16], F32, kind="ExternalInput").ap()

    out_hbm = nc.dram_tensor("out", [T, D], F32, kind="ExternalOutput").ap()

    qp_sp = nc.dram_tensor("qp_sp", [D, T], BF16).ap()
    v_sp = nc.dram_tensor("v_sp", [NOB, T, 512], BF16).ap()
    x2_sp = nc.dram_tensor("x2_sp", [T, D], F32).ap()
    h_sp = nc.dram_tensor("h_sp", [NIC, TC, 128, 128], BF16).ap()
    kv_in = nc.dram_tensor("kv_in", [HEADS * 128, 129], F32).ap()
    kv_out = nc.dram_tensor("kv_out", [HEADS * 128, 129], F32).ap()
    wg_sh = nc.dram_tensor("wg_sh", [2, NIC, 128, 128], BF16).ap()
    wu_sh = nc.dram_tensor("wu_sh", [2, NIC, 128, 128], BF16).ap()
    wd_sh = nc.dram_tensor("wd_sh", [688, D], BF16).ap()
    wg_fu = nc.dram_tensor("wg_fu", [16, NIC, 128, 128], BF16).ap()
    wu_fu = nc.dram_tensor("wu_fu", [16, NIC, 128, 128], BF16).ap()
    wd_fu = nc.dram_tensor("wd_fu", [INTER, D], BF16).ap()

    dbg = {}
    if stage < 99:
        def dbgt(name, shape, dtype):
            dbg[name] = nc.dram_tensor("d_" + name, shape, dtype,
                                       kind="ExternalOutput").ap()
        dbgt("xlnT", [128, JC * T], BF16)
        dbgt("stats", [128, 96], mybir.dt.float32)
        dbgt("wstr", [128, 16384], BF16)
        dbgt("kpb", [128, 8192], BF16)
        dbgt("varena", [128, 8192], BF16)
        dbgt("kvin", [HEADS * 128, 129], mybir.dt.float32)
        dbgt("vsp", [NOB, T, 512], BF16)
        dbgt("kvf", [128, HEADS * 129], F32)
        dbgt("qp", [D, T], BF16)
        dbgt("attn", [128, HEADS * T], BF16)
        dbgt("x2", [T, D], F32)
        dbgt("xln2T", [128, JC * T], BF16)
        dbgt("wgf", [16, NIC, 128, 128], BF16)
        dbgt("wdf", [INTER, D], BF16)
        dbgt("h", [NIC, TC, 128, 128], BF16)

    ident_c = nc.inline_tensor(np.eye(128, dtype=ml_dtypes.bfloat16),
                               name="c_ident").ap()
    onesr_c = nc.inline_tensor(np.ones((1, 128), dtype=ml_dtypes.bfloat16),
                               name="c_onesr").ap()
    onesc_c = nc.inline_tensor(np.ones((128, 1), dtype=ml_dtypes.bfloat16),
                               name="c_onesc").ap()
    vsplat_c = nc.inline_tensor(
        np.tile(np.arange(16, dtype=np.float32)[None, :], (128, 1)),
        name="c_vsplat").ap()

    st = contextlib.ExitStack()
    with st:
        def sb(name, shape, dtype):
            return st.enter_context(nc.sbuf_tensor(name, shape, dtype))

        big0 = sb("big0", [128, JC * T], BF16)
        wstr = sb("wstr", [128, 2 * 8192], BF16)
        xbuf = sb("xbuf", [128, 2 * D], F32)
        kpb = sb("kpb", [128, 8192], BF16)
        varena = sb("varena", [128, 8192], BF16)
        evac = sb("evac", [128, 4096], BF16)
        outb = sb("outb", [128, 2048], F32)
        identt = sb("identt", [128, 128], BF16)
        onesr = sb("onesr", [1, 128], BF16)
        onesc = sb("onesc", [128, 1], BF16)
        qb_sb = sb("qb_sb", [128, 16], F32)
        kb_row = sb("kb_row", [1, D], BF16)
        ob_row = sb("ob_row", [1, D], BF16)
        lut_sp = sb("lut_sp", [128, 16], F32)
        vs_sp = sb("vs_sp", [128, 16], F32)
        sumx = sb("sumx", [128, 16], F32)
        sumsq = sb("sumsq", [128, 16], F32)
        meanb = sb("meanb", [128, 16], F32)
        varb = sb("varb", [128, 16], F32)
        rstdb = sb("rstdb", [128, 16], F32)
        msqb = sb("msqb", [128, 16], F32)
        vfinb = sb("vfinb", [128, 16], F32)
        mbb = sb("mbb", [128, 16], F32)
        kvf = sb("kvf", [128, HEADS * 129], F32)
        kvb = sb("kvb", [128, HEADS * 129], BF16)
        ibuf = sb("b_ibuf", [128, 2816], I32)
        tbuf = sb("b_tbuf", [128, 2816], BF16)
        abuf = sb("b_abuf", [128, 2816], BF16)

        sem_st = {}
        for name in ["ld", "g", "t", "v", "a", "st", "cc", "ld2"]:
            sem_st[name] = st.enter_context(nc.semaphore("s_" + name))
        cnt = {k: 0 for k in sem_st}

        def inc(inst, k, amt):
            inst.then_inc(sem_st[k], amt)
            cnt[k] += amt
            return cnt[k]

        def W(eng, k, val):
            eng.wait_ge(sem_st[k], val)

        block = st.enter_context(nc.Block())

        def big0_stat(jc, tc):
            return big0[:, jc * T + tc * 128: jc * T + (tc + 1) * 128]

        def big0_mv(jc, lo, w):
            return big0[:, jc * T + lo: jc * T + lo + w]

        def wpar(par):
            return wstr[:, par * 8192:(par + 1) * 8192]

        def xtile(par):
            return xbuf[:, par * D:(par + 1) * D]

        def ev(lo, w):
            return evac[:, lo:lo + w]

        kvf3 = kvf.rearrange("p (h e) -> p h e", h=HEADS)
        kvb3 = kvb.rearrange("p (h e) -> p h e", h=HEADS)
        wg_sh2 = wg_sh.rearrange("a b c d -> (a b c) d")
        wu_sh2 = wu_sh.rearrange("a b c d -> (a b c) d")
        wg_fu2 = wg_fu.rearrange("a b c d -> (a b c) d")
        wu_fu2 = wu_fu.rearrange("a b c d -> (a b c) d")

        @block.sync
        def _(sync):
            sync.dma_start(out=identt[:], in_=ident_c).then_inc(sem_st["ld"], 16)
            sync.dma_start(out=onesr[:], in_=onesr_c).then_inc(sem_st["ld"], 16)
            sync.dma_start(out=onesc[:], in_=onesc_c).then_inc(sem_st["ld"], 16)
            sync.dma_start(out=vs_sp[:], in_=vsplat_c).then_inc(sem_st["ld"], 16)
            for oc in range(16):
                sync.dma_start(
                    out=qb_sb[:, oc:oc + 1],
                    in_=qb_h[oc * 128:(oc + 1) * 128].unsqueeze(1),
                ).then_inc(sem_st["ld"], 16)
        cnt["ld"] += 16 * 20
        CONST_LD = cnt["ld"]

        @block.gpsimd
        def _(g):
            g.dma_start(out=kb_row[:], in_=kb_h.unsqueeze(0)).then_inc(sem_st["g"], 16)
            g.dma_start(out=ob_row[:], in_=ob_h.unsqueeze(0)).then_inc(sem_st["g"], 16)
            g.dma_start(out=lut_sp[:],
                        in_=lut_h.unsqueeze(0).broadcast_to([128, 16])
                        ).then_inc(sem_st["g"], 16)
        cnt["g"] += 48
        CONST_G = cnt["g"]

        GCOL = [(0, 2816), (2816, 2688)]
        build_units = []
        for src, shv in ((gi_h, wg_sh), (ui_h, wu_sh)):
            for rchunk in range(2):
                for c0, cw in GCOL:
                    build_units.append(("gu", src, shv, rchunk, 128, c0, cw))
        for rchunk in range(6):
            rows = 128 if rchunk < 5 else 48
            build_units.append(("d", di_h, wd_sh, rchunk, rows, 0, D))
        NUNITS = len(build_units)

        bst = {"loaded": 0, "built": 0, "ld_marks": [], "bv_marks": []}

        def emit_build_load():
            u = bst["loaded"]
            if u >= NUNITS:
                return
            kind, src, shv, rchunk, rows, c0, cw = build_units[u]

            @block.sync
            def _(sync, u=u, src=src, rchunk=rchunk, rows=rows, c0=c0, cw=cw):
                if u > 0:
                    W(sync, "v", bst["bv_marks"][u - 1])
                sync.dma_start(
                    out=ibuf[0:rows, 0:cw],
                    in_=src[rchunk * 128: rchunk * 128 + rows, c0:c0 + cw],
                ).then_inc(sem_st["ld2"], 16)
            cnt["ld2"] += 16
            bst["ld_marks"].append(cnt["ld2"])
            bst["loaded"] += 1

        def build_slot():
            u = bst["built"]
            if u >= NUNITS:
                return
            kind, src, shv, rchunk, rows, c0, cw = build_units[u]

            @block.vector
            def _(v, u=u, rows=rows, cw=cw):
                W(v, "ld2", bst["ld_marks"][u])
                if u == 0:
                    W(v, "ld", CONST_LD)
                    W(v, "g", CONST_G)
                iv = ibuf[0:rows, 0:cw]
                tv = tbuf[0:rows, 0:cw]
                av = abuf[0:rows, 0:cw]
                inst = None
                for val in range(16):
                    inst0 = v.tensor_scalar(
                        out=(av if val == 0 else tv), in0=iv,
                        scalar1=vs_sp[0:rows, val:val + 1],
                        scalar2=lut_sp[0:rows, val:val + 1],
                        op0=ALU.is_equal, op1=ALU.mult,
                    )
                    if val > 0:
                        inst = v.scalar_tensor_tensor(
                            out=av, in0=tv, scalar=1.0, in1=av,
                            op0=ALU.mult, op1=ALU.add,
                        )
                inst.then_inc(sem_st["v"], 1)
            cnt["v"] += 1
            bst["bv_marks"].append(cnt["v"])
            bmark = cnt["v"]

            @block.sync
            def _(sync, kind=kind, shv=shv, rchunk=rchunk, rows=rows,
                  c0=c0, cw=cw, bmark=bmark):
                W(sync, "v", bmark)
                if kind == "d":
                    sync.dma_start(
                        out=shv[rchunk * 128: rchunk * 128 + rows, :],
                        in_=abuf[0:rows, 0:D],
                    ).then_inc(sem_st["st"], 16)
                else:
                    p0 = c0 // 128
                    npc = cw // 128
                    half = npc // 2
                    for (a, b) in ((0, half), (half, npc)):
                        sync.dma_start(
                            out=shv[rchunk, p0 + a: p0 + b, :, :].rearrange(
                                "ic p t -> p ic t"),
                            in_=abuf[:, a * 128: b * 128].rearrange(
                                "p (ic t) -> p ic t", t=128),
                        ).then_inc(sem_st["st"], 16)
            cnt["st"] += 16 if kind == "d" else 32
            bst["built"] += 1
            emit_build_load()

        emit_build_load()

        def ln_sweep(src_hbm, tag, extra_ld_wait=None, big0_free_t=None,
                     psum_free_v=None):
            junk_sq = outb[:, 0:2048]
            junk_ts = kpb[:, 0:2048]
            with (
                nc.psum_tensor(f"pt0_{tag}", [128, 1024], BF16) as pt0,
                nc.psum_tensor(f"pt1_{tag}", [128, 1024], BF16) as pt1,
            ):
                pts = [pt0, pt1]
                ldm = [0] * TC
                a3m = [0] * TC
                a4m = [0] * TC
                v1m = [0] * TC
                t2m = [0] * TC
                for tc in range(TC):
                    @block.sync
                    def _(sync, tc=tc):
                        if tc >= 2:
                            W(sync, "a", a3m[tc - 2])
                            W(sync, "v", v1m[tc - 2])
                        if extra_ld_wait is not None:
                            W(sync, "st", extra_ld_wait[tc])
                        sync.dma_start(
                            out=xtile(tc % 2),
                            in_=src_hbm[tc * 128:(tc + 1) * 128, :],
                        ).then_inc(sem_st["ld"], 16)
                    cnt["ld"] += 16
                    ldm[tc] = cnt["ld"]

                    @block.vector
                    def _(v, tc=tc):
                        W(v, "ld", ldm[tc])
                        v.tensor_scalar(
                            out=junk_ts, in0=xtile(tc % 2),
                            scalar1=1.0, scalar2=0.0,
                            op0=ALU.mult, op1=ALU.add,
                            accum_out=sumx[:, tc:tc + 1],
                        )
                        v.scalar_tensor_tensor(
                            out=junk_ts, in0=xtile(tc % 2), scalar=1.0,
                            in1=xtile(tc % 2), op0=ALU.mult, op1=ALU.mult,
                            accum_out=sumsq[:, tc:tc + 1],
                        ).then_inc(sem_st["v"], 1)
                        cnt["v"] += 1
                        W(v, "v", cnt["v"])
                        v.tensor_scalar_mul(out=meanb[:, tc:tc + 1],
                                            in0=sumx[:, tc:tc + 1],
                                            scalar1=1.0 / D)
                        v.tensor_scalar_mul(out=varb[:, tc:tc + 1],
                                            in0=sumsq[:, tc:tc + 1],
                                            scalar1=1.0 / D
                                            ).then_inc(sem_st["v"], 1)
                        cnt["v"] += 1
                        W(v, "v", cnt["v"])
                        v.tensor_tensor(out=msqb[:, tc:tc + 1],
                                        in0=meanb[:, tc:tc + 1],
                                        in1=meanb[:, tc:tc + 1], op=ALU.mult
                                        ).then_inc(sem_st["v"], 1)
                        cnt["v"] += 1
                        W(v, "v", cnt["v"])
                        v.tensor_tensor(
                            out=vfinb[:, tc:tc + 1], in0=varb[:, tc:tc + 1],
                            in1=msqb[:, tc:tc + 1], op=ALU.subtract,
                        ).then_inc(sem_st["v"], 1)
                    cnt["v"] += 1
                    v1m[tc] = cnt["v"]

                    @block.scalar
                    def _(sc, tc=tc):
                        W(sc, "v", v1m[tc])
                        sc.activation(out=rstdb[:, tc:tc + 1],
                                      in_=vfinb[:, tc:tc + 1],
                                      func=AF.Ln, bias=np.float32(EPS_LN),
                                      scale=np.float32(1.0)
                                      ).then_inc(sem_st["a"], 1)
                        cnt["a"] += 1
                        W(sc, "a", cnt["a"])
                        sc.activation(out=rstdb[:, tc:tc + 1],
                                      in_=rstdb[:, tc:tc + 1],
                                      func=AF.Exp, scale=np.float32(-0.5)
                                      ).then_inc(sem_st["a"], 1)
                    cnt["a"] += 1
                    a2 = cnt["a"]

                    @block.vector
                    def _(v, tc=tc, a2=a2):
                        W(v, "a", a2)
                        v.scalar_tensor_tensor(
                            out=mbb[:, tc:tc + 1], in0=meanb[:, tc:tc + 1],
                            scalar=-1.0, in1=rstdb[:, tc:tc + 1],
                            op0=ALU.mult, op1=ALU.mult,
                        ).then_inc(sem_st["v"], 1)
                    cnt["v"] += 1
                    v2 = cnt["v"]

                    @block.scalar
                    def _(sc, tc=tc, v2=v2):
                        W(sc, "v", v2)
                        if tc >= 2:
                            W(sc, "t", t2m[tc - 2])
                        sc.activation(
                            out=ev((tc % 2) * 2048, 2048), in_=xtile(tc % 2),
                            func=AF.Identity, scale=rstdb[:, tc:tc + 1],
                            bias=mbb[:, tc:tc + 1],
                        ).then_inc(sem_st["a"], 1)
                    cnt["a"] += 1
                    a3m[tc] = cnt["a"]

                    @block.tensor
                    def _(t, tc=tc):
                        W(t, "a", a3m[tc])
                        if tc == 0:
                            if big0_free_t is not None:
                                W(t, "t", big0_free_t)
                            if psum_free_v is not None:
                                W(t, "v", psum_free_v)
                        if tc >= 1:
                            W(t, "a", a4m[tc - 1])
                        for grp in range(2):
                            pt = pts[grp]
                            for j in range(8):
                                jc = grp * 8 + j
                                inst = t.transpose(
                                    pt[:, j * 128:(j + 1) * 128],
                                    ev((tc % 2) * 2048 + jc * 128, 128),
                                    identt[:],
                                )
                            inst.then_inc(sem_st["t"], 1)
                    cnt["t"] += 2
                    t2m[tc] = cnt["t"]

                    @block.scalar
                    def _(sc, tc=tc):
                        b3 = big0.rearrange("p (jc t) -> p jc t", jc=JC)
                        inst = None
                        for grp in range(2):
                            W(sc, "t", t2m[tc] - 1 + grp)
                            inst = sc.activation(
                                out=b3[:, grp * 8:(grp + 1) * 8,
                                       tc * 128:(tc + 1) * 128],
                                in_=pts[grp][:].rearrange(
                                    "p (jc t) -> p jc t", jc=8),
                                func=AF.Identity,
                            )
                        inst.then_inc(sem_st["a"], 1)
                    cnt["a"] += 1
                    a4m[tc] = cnt["a"]
                return a4m[-1], v1m[-1]

        LN1_A, _ = ln_sweep(x_hbm, "ln1")
        build_slot()

        if stage == 1:
            @block.sync
            def _(sync):
                W(sync, "a", LN1_A)
                sync.dma_start(out=dbg["xlnT"], in_=big0[:]).then_inc(
                    sem_st["st"], 16)
                cnt["st"] += 16
                for i, tile in enumerate([sumx, sumsq, meanb, vfinb, rstdb, mbb]):
                    sync.dma_start(out=dbg["stats"][:, i * 16:(i + 1) * 16],
                                   in_=tile[:]).then_inc(sem_st["st"], 16)
                    cnt["st"] += 16
                sync.wait_ge(sem_st["st"], cnt["st"])
            return nc

        wuser = [0, 0]
        wld = [0, 0]

        def stream_load(par, dst, src):
            @block.gpsimd
            def _(g, par=par):
                if wuser[par]:
                    W(g, "t", wuser[par])
                g.dma_start(
                    out=dst, in_=src.rearrange("jc p o -> p jc o"),
                ).then_inc(sem_st["g"], 16)
            cnt["g"] += 16
            wld[par] = cnt["g"]

        with (
            nc.psum_tensor("pv0", [128, 512], F32) as pv0,
            nc.psum_tensor("pv1", [128, 512], F32) as pv1,
        ):
            pvs = [pv0, pv1]
            psum_free = [0, 0]
            vst = []
            st_marks = []
            for ob in range(NOB):
                par = ob % 2
                stream_load(
                    par, wpar(par).rearrange("p (jc o) -> p jc o", jc=JC),
                    vw_t[ob])
                for tc in range(TC):
                    @block.tensor
                    def _(t, ob=ob, tc=tc, par=par):
                        if tc == 0:
                            W(t, "g", wld[par])
                            if ob == 0:
                                W(t, "a", LN1_A)
                        if psum_free[tc % 2]:
                            W(t, "v", psum_free[tc % 2])
                        wv = wpar(par).rearrange("p (jc o) -> p jc o", jc=JC)
                        for jc in range(JC):
                            inst = t.matmul(
                                pvs[tc % 2][:, 0:512], big0_stat(jc, tc),
                                wv[:, jc, :], start=(jc == 0),
                                stop=(jc == JC - 1),
                            )
                        inst.then_inc(sem_st["t"], 1)
                    cnt["t"] += 1
                    tmark = cnt["t"]

                    @block.vector
                    def _(v, tc=tc, tmark=tmark):
                        W(v, "t", tmark)
                        if len(st_marks) >= 2:
                            W(v, "st", st_marks[-2])
                        v.tensor_copy(
                            out=varena[:, (tc % 2) * 512:(tc % 2) * 512 + 512],
                            in_=pvs[tc % 2][:, 0:512],
                        ).then_inc(sem_st["v"], 1)
                    cnt["v"] += 1
                    psum_free[tc % 2] = cnt["v"]
                    vm = cnt["v"]

                    @block.sync
                    def _(sync, ob=ob, tc=tc, vm=vm):
                        W(sync, "v", vm)
                        sync.dma_start(
                            out=v_sp[ob, tc * 128:(tc + 1) * 128, :],
                            in_=varena[:, (tc % 2) * 512:(tc % 2) * 512 + 512],
                        ).then_inc(sem_st["st"], 16)
                    cnt["st"] += 16
                    st_marks.append(cnt["st"])
                wuser[par] = cnt["t"]
                vst.append(cnt["st"])
            V_LAST_EVAC_V = cnt["v"]

        if stage == 15:
            @block.sync
            def _(sync):
                W(sync, "st", cnt["st"])
                sync.dma_start(out=dbg["vsp"], in_=v_sp).then_inc(
                    sem_st["st"], 16)
                cnt["st"] += 16
                sync.wait_ge(sem_st["st"], cnt["st"])
            return nc

        with (
            nc.psum_tensor("pk0", [128, 512], F32) as pk0,
            nc.psum_tensor("pk1", [128, 512], F32) as pk1,
            nc.psum_tensor("pkv0", [128, 129], F32) as pkv0,
            nc.psum_tensor("pkv1", [128, 129], F32) as pkv1,
            nc.psum_tensor("pkv2", [128, 129], F32) as pkv2,
            nc.psum_tensor("pkv3", [128, 129], F32) as pkv3,
        ):
            pks = [pk0, pk1]
            pkvs = [pkv0, pkv1, pkv2, pkv3]
            psum_free = [V_LAST_EVAC_V, V_LAST_EVAC_V]
            kv_evac_v = 0
            varena_free_t = 0
            emark = [0, 0]
            for ob in range(NOB):
                par = ob % 2
                stream_load(
                    par, wpar(par).rearrange("p (jc o) -> p jc o", jc=JC),
                    kw_t[ob])

                @block.sync
                def _(sync, ob=ob):
                    W(sync, "st", vst[-1])
                    if varena_free_t:
                        W(sync, "t", varena_free_t)
                    for tc in range(TC):
                        sync.dma_start(
                            out=varena[:, tc * 512:(tc + 1) * 512],
                            in_=v_sp[ob, tc * 128:(tc + 1) * 128, :],
                        ).then_inc(sem_st["ld"], 16)
                cnt["ld"] += 16 * TC
                vres_ld = cnt["ld"]

                kpmarks = [0] * TC
                for tc in range(TC):
                    @block.tensor
                    def _(t, ob=ob, tc=tc, par=par):
                        if tc == 0:
                            W(t, "g", wld[par])
                            if ob == 0:
                                W(t, "v", V_LAST_EVAC_V)
                        if psum_free[tc % 2]:
                            W(t, "v", psum_free[tc % 2])
                        wv = wpar(par).rearrange("p (jc o) -> p jc o", jc=JC)
                        for jc in range(JC):
                            t.matmul(pks[tc % 2][:, 0:512],
                                     big0_stat(jc, tc), wv[:, jc, :],
                                     start=(jc == 0), stop=(jc == JC - 1))
                        t.matmul(
                            pks[tc % 2][:, 0:512], onesr[:],
                            kb_row[0:1, ob * 512:(ob + 1) * 512],
                            start=False, stop=True, skip_group_check=True,
                        ).then_inc(sem_st["t"], 1)
                    cnt["t"] += 1
                    tmark = cnt["t"]

                    s0 = (tc % 2) * 1536

                    @block.vector
                    def _(v, tc=tc, s0=s0, tmark=tmark):
                        W(v, "t", tmark)
                        if emark[tc % 2]:
                            W(v, "a", emark[tc % 2])
                        v.tensor_scalar_min(out=ev(s0, 512),
                                            in0=pks[tc % 2][:, 0:512],
                                            scalar1=0.0)
                        v.tensor_scalar_max(
                            out=ev(s0 + 512, 512),
                            in0=pks[tc % 2][:, 0:512], scalar1=0.0,
                        ).then_inc(sem_st["v"], 1)
                    cnt["v"] += 1
                    psum_free[tc % 2] = cnt["v"]
                    mrmark = cnt["v"]

                    @block.scalar
                    def _(sc, s0=s0, mrmark=mrmark):
                        W(sc, "v", mrmark)
                        sc.activation(out=ev(s0 + 1024, 512),
                                      in_=ev(s0, 512), func=AF.Exp,
                                      ).then_inc(sem_st["a"], 1)
                    cnt["a"] += 1
                    emark[tc % 2] = cnt["a"]
                    am = cnt["a"]

                    @block.vector
                    def _(v, tc=tc, s0=s0, am=am):
                        W(v, "a", am)
                        v.tensor_tensor(
                            out=kpb[:, tc * 512:(tc + 1) * 512],
                            in0=ev(s0 + 1024, 512), in1=ev(s0 + 512, 512),
                            op=ALU.add,
                        ).then_inc(sem_st["v"], 1)
                    cnt["v"] += 1
                    kpmarks[tc] = cnt["v"]

                @block.tensor
                def _(t, ob=ob):
                    W(t, "ld", vres_ld)
                    if kv_evac_v:
                        W(t, "v", kv_evac_v)
                    va = varena[:].rearrange("p (tc o) -> p tc o", tc=TC)
                    for tc in range(TC):
                        W(t, "v", kpmarks[tc])
                        for hl in range(4):
                            t.matmul(
                                pkvs[hl][:, 0:128],
                                kpb[:, tc * 512 + hl * 128:
                                    tc * 512 + (hl + 1) * 128],
                                va[:, tc, hl * 128:(hl + 1) * 128],
                                start=(tc == 0), stop=(tc == TC - 1),
                            )
                            inst = t.matmul(
                                pkvs[hl][:, 128:129],
                                kpb[:, tc * 512 + hl * 128:
                                    tc * 512 + (hl + 1) * 128],
                                onesc[:, 0:1],
                                start=False, stop=(tc == TC - 1),
                                skip_group_check=True,
                            )
                    inst.then_inc(sem_st["t"], 1)
                cnt["t"] += 1
                varena_free_t = cnt["t"]
                kvt = cnt["t"]
                wuser[par] = cnt["t"]

                @block.vector
                def _(v, ob=ob, kvt=kvt):
                    W(v, "t", kvt)
                    inst = None
                    for hl in range(4):
                        inst = v.tensor_copy(
                            out=kvf3[:, ob * 4 + hl, :], in_=pkvs[hl][:, 0:129])
                    inst.then_inc(sem_st["v"], 1)
                cnt["v"] += 1
                kv_evac_v = cnt["v"]
            KV_EVAC_V = kv_evac_v

        @block.gpsimd
        def _(g):
            W(g, "v", KV_EVAC_V)
            g.dma_start(
                out=kv_in.rearrange("(h d) e -> d h e", h=HEADS),
                in_=kvf3[:],
            ).then_inc(sem_st["g"], 16)
            cnt["g"] += 16
            g.wait_ge(sem_st["g"], cnt["g"])
            g.collective_compute(
                "AllReduce", ALU.add,
                replica_groups=[[0, 1, 2, 3], [4, 5, 6, 7]],
                ins=[kv_in], outs=[kv_out],
            ).then_inc(sem_st["cc"], 1)
        cnt["cc"] += 1
        AR_CC = cnt["cc"]

        with (
            nc.psum_tensor("pq0", [128, 2048], F32) as pq0,
            nc.psum_tensor("pq1", [128, 2048], F32) as pq1,
        ):
            pqs = [pq0, pq1]
            psum_free = [KV_EVAC_V, KV_EVAC_V]
            qp_st = [0] * 16
            for oc in range(16):
                par = oc % 2
                stream_load(
                    par,
                    wpar(par)[:, 0:2048].rearrange("p (jc o) -> p jc o", jc=JC),
                    qw_t[oc])

                @block.tensor
                def _(t, oc=oc, par=par):
                    W(t, "g", wld[par])
                    if psum_free[par]:
                        W(t, "v", psum_free[par])
                    wv = wpar(par)[:, 0:2048].rearrange(
                        "p (jc o) -> p jc o", jc=JC)
                    inst = None
                    for jc in range(JC):
                        for tg in range(4):
                            inst = t.matmul(
                                pqs[par][:, tg * 512:(tg + 1) * 512],
                                wv[:, jc, :],
                                big0_mv(jc, tg * 512, 512),
                                start=(jc == 0), stop=(jc == JC - 1),
                            )
                    inst.then_inc(sem_st["t"], 1)
                cnt["t"] += 1
                wuser[par] = cnt["t"]
                qt = cnt["t"]

                @block.vector
                def _(v, oc=oc, par=par, qt=qt):
                    W(v, "t", qt)
                    if oc >= 1:
                        W(v, "st", qp_st[oc - 1])
                    v.tensor_scalar(
                        out=ev(0, 2048), in0=pqs[par][:],
                        scalar1=qb_sb[:, oc:oc + 1], scalar2=0.0,
                        op0=ALU.add, op1=ALU.min,
                    )
                    v.tensor_scalar(
                        out=ev(2048, 2048), in0=pqs[par][:],
                        scalar1=qb_sb[:, oc:oc + 1], scalar2=0.0,
                        op0=ALU.add, op1=ALU.max,
                    ).then_inc(sem_st["v"], 1)
                cnt["v"] += 1
                psum_free[par] = cnt["v"]
                qm = cnt["v"]

                @block.scalar
                def _(sc, qm=qm):
                    W(sc, "v", qm)
                    sc.activation(out=varena[:, 0:2048], in_=ev(0, 2048),
                                  func=AF.Exp).then_inc(sem_st["a"], 1)
                cnt["a"] += 1
                qa = cnt["a"]

                @block.vector
                def _(v, qa=qa):
                    W(v, "a", qa)
                    v.tensor_tensor(out=ev(0, 2048), in0=varena[:, 0:2048],
                                    in1=ev(2048, 2048), op=ALU.add,
                                    ).then_inc(sem_st["v"], 1)
                cnt["v"] += 1
                qv = cnt["v"]

                @block.sync
                def _(sync, oc=oc, qv=qv):
                    W(sync, "v", qv)
                    sync.dma_start(
                        out=qp_sp[oc * 128:(oc + 1) * 128, :], in_=ev(0, 2048),
                    ).then_inc(sem_st["st"], 16)
                cnt["st"] += 16
                qp_st[oc] = cnt["st"]
                build_slot()
            Q_EVAC_V = cnt["v"]
            Q_MM_T = cnt["t"]

        while bst["built"] < NUNITS:
            build_slot()
        BUILD_ST = cnt["st"]

        @block.gpsimd
        def _(g):
            W(g, "st", BUILD_ST)
            g.collective_compute(
                "AllGather", ALU.bypass,
                replica_groups=[[0, 1, 2, 3, 4, 5, 6, 7]],
                ins=[wg_sh2], outs=[wg_fu2],
            ).then_inc(sem_st["cc"], 1)
            g.collective_compute(
                "AllGather", ALU.bypass,
                replica_groups=[[0, 1, 2, 3, 4, 5, 6, 7]],
                ins=[wu_sh2], outs=[wu_fu2],
            ).then_inc(sem_st["cc"], 1)
            g.collective_compute(
                "AllGather", ALU.bypass,
                replica_groups=[[0, 1, 2, 3, 4, 5, 6, 7]],
                ins=[wd_sh], outs=[wd_fu],
            ).then_inc(sem_st["cc"], 1)
        cnt["cc"] += 3
        AG_CC = cnt["cc"]

        if stage == 2:
            @block.sync
            def _(sync):
                W(sync, "cc", AR_CC)
                W(sync, "st", qp_st[15])
                sync.dma_start(out=dbg["wstr"], in_=wstr[:]).then_inc(
                    sem_st["st"], 16)
                cnt["st"] += 16
                sync.dma_start(out=dbg["kpb"], in_=kpb[:]).then_inc(
                    sem_st["st"], 16)
                cnt["st"] += 16
                sync.dma_start(out=dbg["varena"], in_=varena[:]).then_inc(
                    sem_st["st"], 16)
                cnt["st"] += 16
                sync.dma_start(out=dbg["kvin"], in_=kv_in).then_inc(
                    sem_st["st"], 16)
                cnt["st"] += 16
                sync.dma_start(
                    out=dbg["kvf"].rearrange("p (h e) -> p h e", h=HEADS),
                    in_=kv_out.rearrange("(h d) e -> d h e", h=HEADS),
                ).then_inc(sem_st["st"], 16)
                cnt["st"] += 16
                sync.dma_start(out=dbg["qp"], in_=qp_sp).then_inc(
                    sem_st["st"], 16)
                cnt["st"] += 16
                sync.wait_ge(sem_st["st"], cnt["st"])
            return nc

        vb_bb = kpb[:, 4096:6144]

        @block.gpsimd
        def _(g):
            g.dma_start(out=vb_bb,
                        in_=vb_h.unsqueeze(0).broadcast_to([128, D])
                        ).then_inc(sem_st["g"], 16)
        cnt["g"] += 16
        VB_G = cnt["g"]

        @block.sync
        def _(sync):
            W(sync, "cc", AR_CC)
            sync.dma_start(
                out=kvf3[:], in_=kv_out.rearrange("(h d) e -> d h e", h=HEADS),
            ).then_inc(sem_st["ld"], 16)
        cnt["ld"] += 16
        KVF_LD = cnt["ld"]

        @block.vector
        def _(v):
            W(v, "ld", KVF_LD)
            W(v, "g", VB_G)
            inst = None
            for h in range(HEADS):
                inst = v.scalar_tensor_tensor(
                    out=kvf3[:, h, 0:128],
                    in0=vb_bb[:, h * 128:(h + 1) * 128],
                    scalar=kvf3[:, h, 128:129],
                    in1=kvf3[:, h, 0:128],
                    op0=ALU.mult, op1=ALU.add,
                )
            inst.then_inc(sem_st["v"], 1)
            cnt["v"] += 1
            W(v, "v", cnt["v"])
            v.tensor_copy(out=kvb[:], in_=kvf[:]).then_inc(sem_st["v"], 1)
        cnt["v"] += 1
        KVB_V = cnt["v"]

        with (
            nc.psum_tensor("pz0", [1, 512], F32) as pz0,
            nc.psum_tensor("pz1", [1, 512], F32) as pz1,
            nc.psum_tensor("pr0", [128, 512], F32) as pr0,
            nc.psum_tensor("pr1", [128, 512], F32) as pr1,
            nc.psum_tensor("pa", [128, 2048], F32) as pa,
        ):
            pzs, prs = [pz0, pz1], [pr0, pr1]
            zrow = outb[0:1, 0:2048]
            recb_row = evac[0:1, 0:2048]
            recbb = evac[:, 2048:4096]
            attn_ev = {}
            rec_v = {}
            rbe_a = {}
            qps_v = {}
            qld = [0] * HEADS
            z_t = {}
            rb_t = {}
            at_t = {}
            for h in range(HEADS):
                @block.sync
                def _(sync, h=h):
                    W(sync, "st", qp_st[h])
                    if h >= 2:
                        W(sync, "t", at_t[(h - 2, 3)])
                    sync.dma_start(
                        out=kpb[:, (h % 2) * 2048:(h % 2) * 2048 + 2048],
                        in_=qp_sp[h * 128:(h + 1) * 128, :],
                    ).then_inc(sem_st["ld"], 16)
                cnt["ld"] += 16
                qld[h] = cnt["ld"]

                for tb in range(4):
                    @block.tensor
                    def _(t, h=h, tb=tb):
                        if h == 0 and tb == 0:
                            W(t, "v", Q_EVAC_V)
                            W(t, "v", KVB_V)
                        W(t, "ld", qld[h])
                        if (h, tb - 2) in rec_v:
                            W(t, "v", rec_v[(h, tb - 2)])
                        elif tb < 2 and h >= 1 and (h - 1, tb + 2) in rec_v:
                            W(t, "v", rec_v[(h - 1, tb + 2)])
                        t.matmul(
                            pzs[tb % 2][0:1, 0:512],
                            kvb3[:, h, 128:129],
                            kpb[:, (h % 2) * 2048 + tb * 512:
                                (h % 2) * 2048 + (tb + 1) * 512],
                            start=True, stop=True,
                        ).then_inc(sem_st["t"], 1)
                    cnt["t"] += 1
                    z_t[(h, tb)] = cnt["t"]

                    @block.vector
                    def _(v, h=h, tb=tb):
                        W(v, "t", z_t[(h, tb)])
                        v.tensor_scalar_add(out=zrow[0:1, tb * 512:(tb + 1) * 512],
                                            in0=pzs[tb % 2][0:1, 0:512],
                                            scalar1=EPS_Z
                                            ).then_inc(sem_st["v"], 1)
                        cnt["v"] += 1
                        W(v, "v", cnt["v"])
                        with nc.allow_low_precision(
                                reason="1/z feeds a bf16 matmul operand"):
                            v.reciprocal(
                                out=recb_row[0:1, tb * 512:(tb + 1) * 512],
                                in_=zrow[0:1, tb * 512:(tb + 1) * 512],
                            ).then_inc(sem_st["v"], 1)
                    cnt["v"] += 1
                    rec_v[(h, tb)] = cnt["v"]

                    @block.tensor
                    def _(t, h=h, tb=tb):
                        W(t, "v", rec_v[(h, tb)])
                        if (h, tb - 2) in rbe_a:
                            W(t, "a", rbe_a[(h, tb - 2)])
                        elif tb < 2 and (h - 1, tb + 2) in rbe_a:
                            W(t, "a", rbe_a[(h - 1, tb + 2)])
                        t.matmul(
                            prs[tb % 2][:, 0:512], onesr[:],
                            recb_row[0:1, tb * 512:(tb + 1) * 512],
                            start=True, stop=True,
                        ).then_inc(sem_st["t"], 1)
                    cnt["t"] += 1
                    rb_t[(h, tb)] = cnt["t"]

                    @block.scalar
                    def _(sc, h=h, tb=tb):
                        W(sc, "t", rb_t[(h, tb)])
                        sc.activation(
                            out=recbb[:, tb * 512:(tb + 1) * 512],
                            in_=prs[tb % 2][:, 0:512], func=AF.Identity,
                        ).then_inc(sem_st["a"], 1)
                    cnt["a"] += 1
                    rbe_a[(h, tb)] = cnt["a"]

                    @block.vector
                    def _(v, h=h, tb=tb):
                        W(v, "a", rbe_a[(h, tb)])
                        v.tensor_tensor(
                            out=varena[:, (h % 2) * 2048 + tb * 512:
                                       (h % 2) * 2048 + (tb + 1) * 512],
                            in0=kpb[:, (h % 2) * 2048 + tb * 512:
                                    (h % 2) * 2048 + (tb + 1) * 512],
                            in1=recbb[:, tb * 512:(tb + 1) * 512],
                            op=ALU.mult,
                        ).then_inc(sem_st["v"], 1)
                    cnt["v"] += 1
                    qps_v[(h, tb)] = cnt["v"]

                    @block.tensor
                    def _(t, h=h, tb=tb):
                        W(t, "v", qps_v[(h, tb)])
                        if (h - 1, tb) in attn_ev:
                            W(t, "a", attn_ev[(h - 1, tb)])
                        t.matmul(
                            pa[:, tb * 512:(tb + 1) * 512],
                            kvb3[:, h, 0:128],
                            varena[:, (h % 2) * 2048 + tb * 512:
                                   (h % 2) * 2048 + (tb + 1) * 512],
                            start=True, stop=True,
                        ).then_inc(sem_st["t"], 1)
                    cnt["t"] += 1
                    at_t[(h, tb)] = cnt["t"]

                    @block.scalar
                    def _(sc, h=h, tb=tb):
                        W(sc, "t", at_t[(h, tb)])
                        sc.activation(
                            out=big0[:, h * 2048 + tb * 512:
                                     h * 2048 + (tb + 1) * 512],
                            in_=pa[:, tb * 512:(tb + 1) * 512],
                            func=AF.Identity,
                        ).then_inc(sem_st["a"], 1)
                    cnt["a"] += 1
                    attn_ev[(h, tb)] = cnt["a"]
            ATTN_A = cnt["a"]

        if stage == 4:
            @block.sync
            def _(sync):
                W(sync, "a", ATTN_A)
                sync.dma_start(out=dbg["attn"], in_=big0[:]).then_inc(
                    sem_st["st"], 16)
                cnt["st"] += 16
                sync.wait_ge(sem_st["st"], cnt["st"])
            return nc

        with (
            nc.psum_tensor("po0", [128, 512], F32) as po0,
            nc.psum_tensor("po1", [128, 512], F32) as po1,
        ):
            pos = [po0, po1]
            psum_free = [0, 0]
            x2st = [[0] * TC for _ in range(NOB)]
            xld = {}
            for ob in range(NOB):
                par = ob % 2
                stream_load(
                    par, wpar(par).rearrange("p (jc o) -> p jc o", jc=JC),
                    ow_t[ob])
                for tc in range(TC):
                    @block.sync
                    def _(sync, ob=ob, tc=tc):
                        n = ob * TC + tc
                        if n >= 2:
                            pn = n - 2
                            W(sync, "v", xfree[pn % 2])
                        sync.dma_start(
                            out=xbuf[:, (tc % 2) * 512 + 1024:
                                     (tc % 2) * 512 + 1536],
                            in_=x_hbm[tc * 128:(tc + 1) * 128,
                                      ob * 512:(ob + 1) * 512],
                        ).then_inc(sem_st["ld"], 16)
                    cnt["ld"] += 16
                    xld[(ob, tc)] = cnt["ld"]
                    if ob == 0 and tc == 0:
                        xfree = [0, 0]

                    @block.tensor
                    def _(t, ob=ob, tc=tc, par=par):
                        if tc == 0:
                            W(t, "g", wld[par])
                            if ob == 0:
                                W(t, "a", ATTN_A)
                        if psum_free[tc % 2]:
                            W(t, "v", psum_free[tc % 2])
                        wv = wpar(par).rearrange("p (jc o) -> p jc o", jc=JC)
                        for jc in range(JC):
                            t.matmul(pos[tc % 2][:, 0:512],
                                     big0_stat(jc, tc), wv[:, jc, :],
                                     start=(jc == 0), stop=(jc == JC - 1))
                        t.matmul(
                            pos[tc % 2][:, 0:512], onesr[:],
                            ob_row[0:1, ob * 512:(ob + 1) * 512],
                            start=False, stop=True, skip_group_check=True,
                        ).then_inc(sem_st["t"], 1)
                    cnt["t"] += 1
                    omark = cnt["t"]

                    @block.vector
                    def _(v, ob=ob, tc=tc, omark=omark):
                        W(v, "t", omark)
                        W(v, "ld", xld[(ob, tc)])
                        n = ob * TC + tc
                        if n >= 2:
                            W(v, "st", x2st[(n - 2) // TC][(n - 2) % TC])
                        v.tensor_tensor(
                            out=outb[:, (tc % 2) * 512:(tc % 2) * 512 + 512],
                            in0=pos[tc % 2][:, 0:512],
                            in1=xbuf[:, (tc % 2) * 512 + 1024:
                                     (tc % 2) * 512 + 1536],
                            op=ALU.add,
                        ).then_inc(sem_st["v"], 1)
                    cnt["v"] += 1
                    psum_free[tc % 2] = cnt["v"]
                    xfree[(ob * TC + tc) % 2] = cnt["v"]
                    ovm = cnt["v"]

                    @block.sync
                    def _(sync, ob=ob, tc=tc, ovm=ovm):
                        W(sync, "v", ovm)
                        sync.dma_start(
                            out=x2_sp[tc * 128:(tc + 1) * 128,
                                      ob * 512:(ob + 1) * 512],
                            in_=outb[:, (tc % 2) * 512:(tc % 2) * 512 + 512],
                        ).then_inc(sem_st["st"], 16)
                    cnt["st"] += 16
                    x2st[ob][tc] = cnt["st"]
                wuser[par] = cnt["t"]
            PC_MM_T = cnt["t"]
            PC_EVAC_V = cnt["v"]

        LN2_A, _ = ln_sweep(
            x2_sp, "ln2",
            extra_ld_wait=[x2st[NOB - 1][TC - 1]] * TC,
            big0_free_t=PC_MM_T, psum_free_v=PC_EVAC_V,
        )

        if stage == 5:
            @block.sync
            def _(sync):
                W(sync, "a", LN2_A)
                sync.dma_start(out=dbg["xln2T"], in_=big0[:]).then_inc(
                    sem_st["st"], 16)
                cnt["st"] += 16
                sync.dma_start(out=dbg["x2"], in_=x2_sp).then_inc(
                    sem_st["st"], 16)
                cnt["st"] += 16
                W(sync, "cc", AG_CC)
                sync.dma_start(out=dbg["wgf"], in_=wg_fu).then_inc(
                    sem_st["st"], 16)
                cnt["st"] += 16
                sync.dma_start(out=dbg["wdf"], in_=wd_fu).then_inc(
                    sem_st["st"], 16)
                cnt["st"] += 16
                sync.wait_ge(sem_st["st"], cnt["st"])
            return nc

        with (
            nc.psum_tensor("pg0", [128, 1024], F32) as pg0,
            nc.psum_tensor("pg1", [128, 1024], F32) as pg1,
            nc.psum_tensor("pu0", [128, 1024], F32) as pu0,
            nc.psum_tensor("pu1", [128, 1024], F32) as pu1,
        ):
            pgs, pus = [pg0, pg1], [pu0, pu1]
            ff_ld = [0, 0]
            ff_user = [wuser[0], wuser[1]]
            sg_free = [0, 0]
            hb_free = [0, 0]
            hb_st = [0, 0]
            h_stores = []
            for ic in range(NIC):
                par = ic % 2

                @block.gpsimd
                def _(g, ic=ic, par=par):
                    if ic == 0:
                        W(g, "cc", AG_CC)
                    if ff_user[par]:
                        W(g, "t", ff_user[par])
                    wv = wpar(par)
                    g.dma_start(
                        out=wv[:, 0:2048].rearrange("p (jc o) -> p jc o", jc=JC),
                        in_=wg_fu[:, ic, :, :].rearrange("jc p o -> p jc o"),
                    ).then_inc(sem_st["g"], 16)
                    g.dma_start(
                        out=wv[:, 2048:4096].rearrange(
                            "p (jc o) -> p jc o", jc=JC),
                        in_=wu_fu[:, ic, :, :].rearrange("jc p o -> p jc o"),
                    ).then_inc(sem_st["g"], 16)
                cnt["g"] += 32
                ff_ld[par] = cnt["g"]

                for th in range(2):
                    @block.tensor
                    def _(t, ic=ic, th=th, par=par):
                        if th == 0:
                            W(t, "g", ff_ld[par])
                            if ic == 0:
                                W(t, "a", LN2_A)
                        if sg_free[th]:
                            W(t, "a", sg_free[th])
                        wv = wpar(par)
                        for jc in range(JC):
                            for sub in range(2):
                                inst = t.matmul(
                                    pgs[th][:, sub * 512:(sub + 1) * 512],
                                    wv[:, jc * 128:(jc + 1) * 128],
                                    big0_mv(jc, th * 1024 + sub * 512, 512),
                                    start=(jc == 0), stop=(jc == JC - 1),
                                )
                        inst.then_inc(sem_st["t"], 1)
                    cnt["t"] += 1
                    gmark = cnt["t"]

                    @block.tensor
                    def _(t, ic=ic, th=th, par=par):
                        if hb_free[th]:
                            W(t, "v", hb_free[th])
                        wv = wpar(par)
                        for jc in range(JC):
                            for sub in range(2):
                                inst = t.matmul(
                                    pus[th][:, sub * 512:(sub + 1) * 512],
                                    wv[:, 2048 + jc * 128:
                                       2048 + (jc + 1) * 128],
                                    big0_mv(jc, th * 1024 + sub * 512, 512),
                                    start=(jc == 0), stop=(jc == JC - 1),
                                )
                        inst.then_inc(sem_st["t"], 1)
                    cnt["t"] += 1
                    umark = cnt["t"]
                    if th == 1:
                        ff_user[par] = cnt["t"]

                    @block.scalar
                    def _(sc, th=th, gmark=gmark):
                        W(sc, "t", gmark)
                        if hb_free[th]:
                            W(sc, "v", hb_free[th])
                        sc.activation(
                            out=ev(th * 2048, 1024), in_=pgs[th][:],
                            func=AF.Silu,
                        ).then_inc(sem_st["a"], 1)
                    cnt["a"] += 1
                    sg_free[th] = cnt["a"]
                    sgm = cnt["a"]

                    @block.vector
                    def _(v, th=th, umark=umark, sgm=sgm):
                        W(v, "t", umark)
                        W(v, "a", sgm)
                        if hb_st[th]:
                            W(v, "st", hb_st[th])
                        v.tensor_tensor(
                            out=ev(th * 2048 + 1024, 1024), in0=pus[th][:],
                            in1=ev(th * 2048, 1024), op=ALU.mult,
                        ).then_inc(sem_st["v"], 1)
                    cnt["v"] += 1
                    hb_free[th] = cnt["v"]
                    hvm = cnt["v"]

                    @block.sync
                    def _(sync, ic=ic, th=th, hvm=hvm):
                        W(sync, "v", hvm)
                        sync.dma_start(
                            out=h_sp[ic, th * 8:(th + 1) * 8, :, :].rearrange(
                                "tc p t -> p tc t"),
                            in_=ev(th * 2048 + 1024, 1024).rearrange(
                                "p (tc t) -> p tc t", t=128),
                        ).then_inc(sem_st["st"], 16)
                    cnt["st"] += 16
                    hb_st[th] = cnt["st"]
                    h_stores.append(cnt["st"])
            PE_MM_T = cnt["t"]
            H_ST = cnt["st"]

        if stage == 7:
            @block.sync
            def _(sync):
                W(sync, "st", H_ST)
                sync.dma_start(out=dbg["h"], in_=h_sp).then_inc(
                    sem_st["st"], 16)
                cnt["st"] += 16
                sync.wait_ge(sem_st["st"], cnt["st"])
            return nc

        with (
            nc.psum_tensor("pd0", [128, 1024], F32) as pd0,
            nc.psum_tensor("pd1", [128, 1024], F32) as pd1,
        ):
            pds = [pd0, pd1]

            def wd_tile(ic):
                if ic < 32:
                    return big0[:, ic * 1024:(ic + 1) * 1024]
                return wstr[:, (ic - 32) * 1024:(ic - 31) * 1024]

            psum_free = [0, 0]
            hld = [0, 0]
            x2ld = [0, 0]
            outfree = [0, 0]
            dn_mm_last = 0
            for odh in range(2):
                @block.sync
                def _(sync, odh=odh):
                    if odh == 0:
                        W(sync, "t", PE_MM_T)
                    else:
                        W(sync, "t", dn_mm_last)
                    for ic in range(NIC):
                        sync.dma_start(
                            out=wd_tile(ic),
                            in_=wd_fu[ic * 128:(ic + 1) * 128,
                                      odh * 1024:(odh + 1) * 1024],
                        ).then_inc(sem_st["ld"], 16)
                cnt["ld"] += 16 * NIC
                wd_ld = cnt["ld"]

                for tc in range(TC):
                    par = tc % 2

                    @block.sync
                    def _(sync, odh=odh, tc=tc, par=par):
                        if odh == 0 and tc == 0:
                            W(sync, "st", H_ST)
                        if hld[par]:
                            W(sync, "t", psum_free_t[par])
                            W(sync, "v", psum_free[par])
                        dst = (kpb[:, 0:5504] if par == 0
                               else varena[:, 0:5504])
                        sync.dma_start(
                            out=dst.rearrange("p (ic t) -> p ic t", t=128),
                            in_=h_sp[:, tc, :, :].rearrange(
                                "ic p t -> p ic t"),
                        ).then_inc(sem_st["ld"], 16)
                        sync.dma_start(
                            out=xbuf[:, par * 1024:(par + 1) * 1024],
                            in_=x2_sp[tc * 128:(tc + 1) * 128,
                                      odh * 1024:(odh + 1) * 1024],
                        ).then_inc(sem_st["ld"], 16)
                    cnt["ld"] += 32
                    hld[par] = cnt["ld"]
                    if odh == 0 and tc == 0:
                        psum_free_t = [0, 0]

                    @block.tensor
                    def _(t, odh=odh, tc=tc, par=par):
                        W(t, "ld", max(hld[par], wd_ld))
                        if psum_free[par]:
                            W(t, "v", psum_free[par])
                        src = (kpb[:, 0:5504] if par == 0
                               else varena[:, 0:5504])
                        for ic in range(NIC):
                            for sub in range(2):
                                inst = t.matmul(
                                    pds[par][:, sub * 512:(sub + 1) * 512],
                                    src[:, ic * 128:(ic + 1) * 128],
                                    wd_tile(ic)[:, sub * 512:(sub + 1) * 512],
                                    start=(ic == 0), stop=(ic == NIC - 1),
                                )
                        inst.then_inc(sem_st["t"], 1)
                    cnt["t"] += 1
                    psum_free_t[par] = cnt["t"]
                    dmark = cnt["t"]
                    dn_mm_last = cnt["t"]

                    @block.vector
                    def _(v, odh=odh, tc=tc, par=par, dmark=dmark):
                        W(v, "t", dmark)
                        W(v, "ld", hld[par])
                        if outfree[par]:
                            W(v, "st", outfree[par])
                        v.tensor_tensor(
                            out=outb[:, par * 1024:(par + 1) * 1024],
                            in0=pds[par][:],
                            in1=xbuf[:, par * 1024:(par + 1) * 1024],
                            op=ALU.add,
                        ).then_inc(sem_st["v"], 1)
                    cnt["v"] += 1
                    psum_free[par] = cnt["v"]
                    dvm = cnt["v"]

                    @block.sync
                    def _(sync, odh=odh, tc=tc, par=par, dvm=dvm):
                        W(sync, "v", dvm)
                        sync.dma_start(
                            out=out_hbm[tc * 128:(tc + 1) * 128,
                                        odh * 1024:(odh + 1) * 1024],
                            in_=outb[:, par * 1024:(par + 1) * 1024],
                        ).then_inc(sem_st["st"], 16)
                    cnt["st"] += 16
                    outfree[par] = cnt["st"]

        @block.sync
        def _(sync):
            sync.wait_ge(sem_st["st"], cnt["st"])

    return nc


_NC_CACHE = {}


def _get_nc(stage=99):
    if stage not in _NC_CACHE:
        _NC_CACHE[stage] = build_nc(stage)
    return _NC_CACHE[stage]


def make_in_maps(inputs):
    x = np.ascontiguousarray(
        np.asarray(inputs["L0_attn_hidden_states"], dtype=np.float32)
    ).reshape(8, T, D)

    def tile_ob(w):
        return np.ascontiguousarray(
            w.T.reshape(JC, 128, NOB, 512).transpose(2, 0, 1, 3))

    qw = np.asarray(inputs["L0_attn_q_w"], np.float32)
    kw = np.asarray(inputs["L0_attn_k_w"], np.float32)
    vw = np.asarray(inputs["L0_attn_v_w"], np.float32)
    ow = np.asarray(inputs["L0_attn_o_w"], np.float32)
    qw_t = np.ascontiguousarray(
        qw.T.reshape(JC, 128, 16, 128).transpose(2, 0, 1, 3))
    kw_t, vw_t, ow_t = tile_ob(kw), tile_ob(vw), tile_ob(ow)

    gi = np.asarray(inputs["L0_ffn_gate_idx"], np.int32).T
    ui = np.asarray(inputs["L0_ffn_up_idx"], np.int32).T
    di = np.asarray(inputs["L0_ffn_down_idx"], np.int32).T
    gi = np.ascontiguousarray(gi)
    ui = np.ascontiguousarray(ui)
    di = np.ascontiguousarray(di)

    common = dict(
        qw_t=qw_t, kw_t=kw_t, vw_t=vw_t, ow_t=ow_t,
        qb=np.asarray(inputs["L0_attn_q_b"], np.float32),
        kb=np.asarray(inputs["L0_attn_k_b"], np.float32),
        vb=np.asarray(inputs["L0_attn_v_b"], np.float32),
        ob=np.asarray(inputs["L0_attn_o_b"], np.float32),
        lut=np.asarray(inputs["L0_ffn_lut"], np.float32),
    )
    in_maps = []
    for c in range(8):
        m = dict(common)
        m["x"] = x[c]
        m["g_idx"] = np.ascontiguousarray(gi[256 * c:256 * (c + 1)])
        m["u_idx"] = np.ascontiguousarray(ui[256 * c:256 * (c + 1)])
        m["d_idx"] = np.ascontiguousarray(di[688 * c:688 * (c + 1)])
        in_maps.append(m)
    return in_maps


def kernel(**inputs):
    nc = _get_nc(99)
    in_maps = make_in_maps(inputs)
    res = run_bass_kernel_spmd(nc, in_maps, core_ids=list(range(8)))
    out = np.concatenate([res.results[c]["out"] for c in range(8)], axis=0)
    return out.reshape(2, 8192, D)
